# version 18
# baseline (speedup 1.0000x reference)
import numpy as np
import ml_dtypes
import concourse.bass as bass
import concourse.mybir as mybir
from concourse.bass_utils import run_bass_kernel_spmd

F32 = mybir.dt.float32
BF16 = mybir.dt.bfloat16
I32 = mybir.dt.int32
I8 = mybir.dt.int8
AF = mybir.ActivationFunctionType
ALU = mybir.AluOpType
AX = mybir.AxisListType

NCORE = 8
S = 1024
HT = 32
IT = 86
GRPS = []
_i = 0
for _g in [3] * 2 + [5] * 16:
    GRPS.append(list(range(_i, _i + _g)))
    _i += _g
assert _i == IT
GMAX = 5
DPOOL = 15
EPS = 1e-5


def build_nc():
    nc = bass.Bass("TRN2", target_bir_lowering=False, debug=False)

    xin = nc.dram_tensor("xin", [HT, 128, S], F32, kind="ExternalInput").ap()
    guw = nc.dram_tensor("guw", [IT, 128, HT, 2, 128], I8,
                         kind="ExternalInput").ap()
    dw = nc.dram_tensor("dw", [IT, 128, HT, 128], I8, kind="ExternalInput").ap()
    aw = nc.dram_tensor("aw", [HT, 128, HT, 128], F32, kind="ExternalInput").ap()
    mk = nc.dram_tensor("mk", [HT, 128, 4096], I8, kind="ExternalInput").ap()
    gs = nc.dram_tensor("gs", [128, IT], F32, kind="ExternalInput").ap()
    us = nc.dram_tensor("us", [128, IT], F32, kind="ExternalInput").ap()
    ds = nc.dram_tensor("ds", [128, HT], F32, kind="ExternalInput").ap()
    bi = nc.dram_tensor("bi", [128, HT], F32, kind="ExternalInput").ap()
    onesb_d = nc.dram_tensor("onesb", [128, 1], BF16, kind="ExternalInput").ap()
    onesr_d = nc.dram_tensor("onesr", [1, 128], BF16, kind="ExternalInput").ap()
    a128_d = nc.dram_tensor("a128", [128, 128], F32, kind="ExternalInput").ap()
    yt = nc.dram_tensor("yt", [HT, 128, S], F32, kind="ExternalOutput").ap()
    wmscr = nc.dram_tensor("wmscr", [HT, 128, 4096], BF16, kind="Internal").ap()

    from contextlib import ExitStack
    with ExitStack() as ctx:
        E = ctx.enter_context
        xT = E(nc.sbuf_tensor("xT", [128, HT, S], BF16))
        lnb = E(nc.sbuf_tensor("lnb", [128, HT, S], BF16))
        pT = E(nc.sbuf_tensor("pT", [128, GMAX, S], BF16))
        gustg = E(nc.sbuf_tensor("gustg", [128, 3, 8, 2, 128], BF16))
        dstg = E(nc.sbuf_tensor("dstg", [128, DPOOL, 4, 128], BF16))
        g_sb = E(nc.sbuf_tensor("g_sb", [128, S], BF16))
        tmp_n = E(nc.sbuf_tensor("tmp_n", [128, S], BF16))
        tmp_e = E(nc.sbuf_tensor("tmp_e", [128, 5, 512], BF16))
        out_sb = E(nc.sbuf_tensor("out_sb", [128, 2, S], F32))
        stats = E(nc.sbuf_tensor("stats", [1, 2048], BF16))
        shp = E(nc.sbuf_tensor("shp", [128, 4096], BF16))
        shpf = E(nc.sbuf_tensor("shpf", [128, 2048], F32))
        onesb = E(nc.sbuf_tensor("onesb_s", [128, 1], BF16))
        onesr = E(nc.sbuf_tensor("onesr_s", [1, 128], BF16))
        a128 = E(nc.sbuf_tensor("a128_s", [128, 128], F32))
        s1f = E(nc.sbuf_tensor("s1f", [128, 2, 32], F32))
        scmb = E(nc.sbuf_tensor("scmb", [128, IT], F32))
        hds = E(nc.sbuf_tensor("hds", [128, HT], F32))
        bi_s = E(nc.sbuf_tensor("bi_s", [128, HT], F32))
        m01 = E(nc.sbuf_tensor("m01", [128, 2, 32], BF16))
        pA = E(nc.psum_tensor("pA", [128, 2048], F32))
        pB = E(nc.psum_tensor("pB", [128, 2048], F32))

        sem_names = (["DV", "AC", "PE", "GP", "XIN", "CST", "WMW0", "WMW1",
                      "G0", "G1", "G2",
                      "MK0", "MK1", "AW0", "AW1",
                      "WS0", "WS1", "WH0", "WH1", "O0", "O1"] + [f"D{i}" for i in range(DPOOL)])
        sems = {n: E(nc.semaphore("s" + n)) for n in sem_names}
        block = E(nc.Block())

        prog = {k: [] for k in "tvsgy"}
        cnt = {"DV": 0, "AC": 0, "PE": 0, "GP": 0}
        fills = {n: 0 for n in sem_names}

        def V(build, waits=()):
            w = tuple([(sems["DV"], cnt["DV"])] +
                      [x for x in waits if x is not None])
            cnt["DV"] += 1

            def f(v, _w=w, _b=build):
                for sh, val in _w:
                    if val > 0:
                        v.wait_ge(sh, val)
                _b(v).then_inc(sems["DV"], 1)
            prog["v"].append(f)
            return cnt["DV"]

        def A(build, waits=()):
            w = tuple([(sems["AC"], cnt["AC"])] +
                      [x for x in waits if x is not None])
            cnt["AC"] += 1

            def f(s, _w=w, _b=build):
                for sh, val in _w:
                    if val > 0:
                        s.wait_ge(sh, val)
                _b(s).then_inc(sems["AC"], 1)
            prog["s"].append(f)
            return cnt["AC"]

        def G(build, waits=()):
            w = tuple([(sems["GP"], cnt["GP"])] +
                      [x for x in waits if x is not None])
            cnt["GP"] += 1

            def f(g, _w=w, _b=build):
                for sh, val in _w:
                    if val > 0:
                        g.wait_ge(sh, val)
                _b(g).then_inc(sems["GP"], 1)
            prog["g"].append(f)
            return cnt["GP"]

        def T(build, waits=(), inc=False):
            if inc:
                cnt["PE"] += 1

            def f(t, _w=tuple(x for x in waits if x is not None),
                  _b=build, _i=inc):
                for sh, val in _w:
                    if val > 0:
                        t.wait_ge(sh, val)
                op = _b(t)
                if _i:
                    op.then_inc(sems["PE"], 1)
            prog["t"].append(f)
            return cnt["PE"]

        def DMA(e, dst, src, sem_name, waits=()):
            fills[sem_name] += 1
            val = 16 * fills[sem_name]

            def f(eng, _w=tuple(x for x in waits if x is not None),
                  _d=dst, _s=src, _n=sem_name):
                for sh, v in _w:
                    if v > 0:
                        eng.wait_ge(sh, v)
                eng.dma_start(out=_d, in_=_s).then_inc(sems[_n], 16)
            prog[e].append(f)
            return (sems[sem_name], val)

        def YW(waits):
            def f(eng, _w=tuple(x for x in waits if x is not None)):
                for sh, v in _w:
                    if v > 0:
                        eng.wait_ge(sh, v)
            prog["y"].append(f)

        w_xin = DMA("g", xT[:, :, :], xin.rearrange("hc p s -> p hc s"), "XIN")
        gs_s = shpf[:, 0:86]
        us_s = shpf[:, 86:172]
        ds_s = shpf[:, 172:204]
        for dst, src in ((onesb[:], onesb_d[:, :]),
                         (onesr[:], onesr_d[:, :]),
                         (a128[:], a128_d[:, :]), (gs_s, gs[:, :]),
                         (us_s, us[:, :]), (ds_s, ds[:, :]),
                         (bi_s[:], bi[:, :])):
            DMA("y", dst, src, "CST")
        w_cst = (sems["CST"], 16 * fills["CST"])
        V(lambda v: v.tensor_tensor(scmb[:], gs_s, us_s, op=ALU.mult),
          waits=[w_cst])
        V(lambda v: v.tensor_scalar(hds[:], ds_s, 0.5, None, op0=ALU.mult))
        dv_consts = cnt["DV"]

        psg = [pA[:, 0:1024], pB[:, 0:1024]]
        psu = [pA[:, 1024:2048], pB[:, 1024:2048]]
        psd = [pA[:, 0:1024], pA[:, 1024:2048]]
        psd4 = [pA[:, 0:512], pA[:, 512:1024], pA[:, 1024:1536],
                pA[:, 1536:2048], pB[:, 512:1024]]
        meanB = pB[:, 0:1024]
        rstdB = pB[:, 1024:2048]
        psSx = pB[0:1, 0:1024]
        psSq = pB[0:1, 1024:2048]
        psM = pB[:, 0:32]

        norm_done = {}
        norm_hc = {}
        gsb_rel = [None]
        tmpe_rel = {0: None, 1: None, 2: None, 3: None, 4: None}
        psd4_free = {0: None, 1: None, 2: None, 3: None, 4: None}
        tile_fin = {}
        psd_free = {0: None, 1: None}
        slot_rel = [None, None, None, None]
        gu_rel = {"G": {}, "U": {}}
        drel = {}

        mkbuf = shp[:, 0:2048].rearrange("p (b n) -> p b n", b=2)
        wmbuf = shp[:, 2048:4096].rearrange("p (b n) -> p b n", b=2)
        awbuf = shpf[:, 0:2048].rearrange("p (b n) -> p b n", b=2)
        mst = {"li": 0, "ci": 0, "mk_rel": {}, "aw_rel": {}, "wm_rel": {},
               "aw_fill": {}, "red_dv": {}, "last_sign_ac": 0, "a128_pe": 0,
               "ac_floor": 0}
        mask_units = [(kt, q) for kt in range(HT) for q in range(4)]
        MLAG = 2

        def emit_mask_load():
            u = mst["li"]
            if u >= len(mask_units):
                return
            mst["li"] += 1
            kt, q = mask_units[u]
            b = u % 2
            w_mk = DMA("g", mkbuf[:, b], mk[kt, :, 1024 * q:1024 * (q + 1)],
                       f"MK{b}",
                       waits=[mst["mk_rel"].get(b),
                              (sems["DV"], dv_consts) if u < 2 else None])
            mst["aw_fill"][u] = DMA(
                "y", awbuf[:, b],
                aw[kt, :, 8 * q:8 * q + 8, :].rearrange("p a b -> p (a b)"),
                f"AW{b}",
                waits=[mst["aw_rel"].get(b),
                       (sems["DV"], dv_consts) if u < 2 else None])
            dv_red = V(lambda v, _m=mkbuf[:, b], _b=b: v.tensor_reduce(
                s1f[:, _b], _m.rearrange("p (a c) -> p a c", c=32),
                axis=AX.X, op=ALU.add),
                waits=[w_mk, (sems["PE"], mst["a128_pe"])])
            mst["mk_rel"][b] = (sems["DV"], dv_red)
            mst["red_dv"][u] = dv_red

        def emit_mask_comp():
            u = mst["ci"]
            if u >= len(mask_units) or u >= mst["li"]:
                return
            mst["ci"] += 1
            kt, q = mask_units[u]
            b = u % 2
            pe_a = T(lambda t, _b=b: t.matmul(psM, a128[:, :], s1f[:, _b],
                                              start=True, stop=True),
                     waits=[(sems["DV"], mst["red_dv"][u]),
                            (sems["AC"], max(mst["last_sign_ac"],
                                             mst["ac_floor"])), w_cst],
                     inc=True)
            mst["a128_pe"] = pe_a
            ac_s = A(lambda s, _b=b: s.activation(m01[:, _b], psM, AF.Sign),
                     waits=[(sems["PE"], pe_a)])
            mst["last_sign_ac"] = ac_s
            m_b = m01[:, b].unsqueeze(2).broadcast_to([128, 32, 32])
            dv_wm = V(lambda v, _b=b, _mb=m_b: v.tensor_tensor(
                wmbuf[:, _b].rearrange("p (a c) -> p a c", c=32),
                awbuf[:, _b].rearrange("p (a c) -> p a c", c=32),
                _mb, op=ALU.mult),
                waits=[mst["aw_fill"][u], (sems["AC"], ac_s),
                       mst["wm_rel"].get(b)])
            mst["aw_rel"][b] = (sems["DV"], dv_wm)
            mst["wm_rel"][b] = DMA(
                "y", wmscr[kt, :, 1024 * q:1024 * (q + 1)], wmbuf[:, b],
                f"WMW{b}", waits=[(sems["DV"], dv_wm)])

        def emit_ln(ln_idx, first=False):
            sq_pe_prev = 0
            for hc in range(HT):
                waits = []
                if hc == 0:
                    waits.append((sems["PE"], mst["a128_pe"]))
                    waits.append((sems["AC"], mst["last_sign_ac"]))
                    if first:
                        waits.append(w_cst)
                if first:
                    waits.append(w_xin)
                elif hc in tile_fin:
                    waits.append(tile_fin[hc])
                T(lambda t, _hc=hc: t.matmul(
                    psSx[:, 0:512], onesb[:, :], xT[:, _hc, 0:512],
                    start=(_hc == 0), stop=(_hc == HT - 1)), waits=waits)
                T(lambda t, _hc=hc: t.matmul(
                    psSx[:, 512:1024], onesb[:, :], xT[:, _hc, 512:1024],
                    start=(_hc == 0), stop=(_hc == HT - 1)))
                sq_w = [(sems["PE"], sq_pe_prev)]
                if first:
                    sq_w.append(w_xin)
                elif hc in tile_fin:
                    sq_w.append(tile_fin[hc])
                ac_sq = A(lambda s, _hc=hc: s.activation(
                    tmp_n[:], xT[:, _hc, :], AF.Square), waits=sq_w)
                T(lambda t, _hc=hc: t.matmul(
                    psSq[:, 0:512], onesb[:, :], tmp_n[:, 0:512],
                    start=(_hc == 0), stop=(_hc == HT - 1)),
                  waits=[(sems["AC"], ac_sq)])
                sq_pe_prev = T(lambda t, _hc=hc: t.matmul(
                    psSq[:, 512:1024], onesb[:, :], tmp_n[:, 512:1024],
                    start=(_hc == 0), stop=(_hc == HT - 1)), inc=True)
            mean_v = stats[0:1, 0:1024]
            rstd_v = stats[0:1, 1024:2048]
            scr0 = out_sb[0:1, 0, :]
            scr1 = out_sb[0:1, 1, :]
            V(lambda v: v.tensor_scalar(mean_v, psSx, 1.0 / 4096, None,
                                        op0=ALU.mult),
              waits=[(sems["PE"], sq_pe_prev)])
            V(lambda v: v.tensor_scalar(scr0, psSq, 1.0 / 4096, None,
                                        op0=ALU.mult))
            V(lambda v: v.tensor_tensor(scr1, mean_v, mean_v, op=ALU.mult))
            V(lambda v: v.tensor_tensor(scr0, scr0, scr1, op=ALU.subtract))
            dv_var = V(lambda v: v.tensor_scalar(scr1, scr0, EPS, None,
                                                 op0=ALU.add))
            ac_std = A(lambda s: s.activation(scr0, scr1, AF.Sqrt),
                       waits=[(sems["DV"], dv_var)])
            V(lambda v: v.reciprocal(scr1, scr0), waits=[(sems["AC"], ac_std)])
            dv_rstd = V(lambda v: v.tensor_copy(rstd_v, scr1))
            pe_bc = 0
            for half in range(2):
                T(lambda t, _h=half: t.matmul(
                    meanB[:, _h * 512:(_h + 1) * 512], onesr[0:1, :],
                    mean_v[:, _h * 512:(_h + 1) * 512], start=True, stop=True),
                  waits=[(sems["DV"], dv_rstd)] if half == 0 else ())
                pe_bc = T(lambda t, _h=half: t.matmul(
                    rstdB[:, _h * 512:(_h + 1) * 512], onesr[0:1, :],
                    rstd_v[:, _h * 512:(_h + 1) * 512], start=True, stop=True),
                    inc=(half == 1))
            mb_sb = dstg[:, 11:13, :, :].rearrange("p b h j -> p (b h j)")
            rs_sb = dstg[:, 13:15, :, :].rearrange("p b h j -> p (b h j)")
            V(lambda v: v.tensor_copy(mb_sb, meanB), waits=[(sems["PE"], pe_bc)])
            V(lambda v: v.tensor_copy(rs_sb, rstdB))
            dvn = 0
            nhc = []
            for hc in range(HT):
                V(lambda v, _hc=hc: v.tensor_tensor(
                    tmp_n[:], xT[:, _hc, :], mb_sb, op=ALU.subtract))
                dvn = V(lambda v, _hc=hc: v.tensor_tensor(
                    lnb[:, _hc, :], tmp_n[:], rs_sb, op=ALU.mult))
                nhc.append(dvn)
            norm_done[ln_idx] = dvn
            norm_hc[ln_idx] = nhc
            tile_fin.clear()

        def emit_ffn(pass_idx, ln_idx, final=False):
            dseq = 0
            dbuf = {}
            dfin = {}
            pTT_by_ktl = {}
            pA_guard = [None, None]
            pB_guard = [None, None]
            guard_of = {}
            slot_seen = set()
            rot = [0]
            out_rel = {}
            pTT_last = 0
            next_ip = 0
            for gi, grp in enumerate(GRPS):
                Gn = len(grp)

                def emit_down_fills(_gi, _grp, _ck):
                    nonlocal dseq
                    for _ktl, _it in enumerate(_grp):
                        b = dseq % DPOOL
                        dbuf[(_gi, _ck, _ktl)] = b
                        w = [drel.get(b)]
                        if dseq < DPOOL and b >= 11:
                            w.append((sems["DV"], norm_done[ln_idx]))
                        dfin[(_gi, _ck, _ktl)] = DMA(
                            "g", dstg[:, b], dw[_it, :, 4 * _ck:4 * (_ck + 1), :],
                            f"D{b}", waits=w)
                        dseq += 1
                if gi > 0:
                    next_ip = 1
                for it in grp:
                    ip = next_ip
                    next_ip = 1 - next_ip
                    pe_g = pe_u = 0
                    chase = (gi == 0 and it == grp[0])
                    for h in range(4):
                        b = (it * 4 + h) % 3
                        w_fill = DMA(
                            "g", gustg[:, b],
                            guw[it, :, 8 * h:8 * (h + 1), :, :],
                            f"G{b}", waits=[gu_rel["G"].get(b)])
                        waits = [w_fill]
                        if h == 0:
                            waits.append(slot_rel[ip * 2])
                            waits.append(slot_rel[ip * 2 + 1])
                            if not chase:
                                waits.append((sems["DV"], norm_done[ln_idx]))
                            waits.append((sems["AC"], mst["last_sign_ac"]))
                        for wname, wi, ps in (("G", 0, psg[ip]),
                                              ("U", 1, psu[ip])):
                            for hcl in range(8):
                                hc = h * 8 + hcl
                                if chase and wname == "G":
                                    waits.append((sems["DV"], norm_hc[ln_idx][hc]))
                                for s2 in range(2):
                                    pe = T(lambda t, _b=b, _hcl=hcl, _wi=wi,
                                           _hc=hc, _s2=s2, _ps=ps: t.matmul(
                                               _ps[:, _s2 * 512:(_s2 + 1) * 512],
                                               gustg[:, _b, _hcl, _wi, :],
                                               lnb[:, _hc, _s2 * 512:(_s2 + 1) * 512],
                                               start=(_hc == 0),
                                               stop=(_hc == HT - 1)),
                                           waits=waits,
                                           inc=(hcl == 7 and s2 == 1))
                                    waits = []
                            if wname == "G":
                                pe_g = pe
                            else:
                                pe_u = pe
                                gu_rel["G"][b] = (sems["PE"], pe)
                    ac_g = A(lambda s, _it=it, _ip=ip: s.activation(
                        g_sb[:], psg[_ip], AF.Copy, scale=scmb[:, _it:_it + 1]),
                        waits=[(sems["PE"], pe_g), (sems["DV"], dv_consts),
                               gsb_rel[0]])
                    ktl = grp.index(it)
                    dv_p = V(lambda v, _ip=ip, _k=ktl: v.tensor_tensor(
                        pT[:, _k, :], g_sb[:], psu[_ip], op=ALU.mult),
                        waits=[(sems["AC"], ac_g), (sems["PE"], pe_u)])
                    pTT_last = dv_p
                    pTT_by_ktl[ktl] = dv_p
                    if ip == 0:
                        pA_guard[0] = (sems["AC"], ac_g)
                        pA_guard[1] = (sems["DV"], dv_p)
                    else:
                        pB_guard[0] = (sems["AC"], ac_g)
                        pB_guard[1] = (sems["DV"], dv_p)
                    guard_of = {0: pA_guard[0], 1: pA_guard[0],
                                2: pA_guard[1], 3: pA_guard[1],
                                4: pB_guard[0]}
                    gsb_rel[0] = (sems["DV"], dv_p)
                    slot_rel[ip * 2] = (sems["AC"], ac_g)
                    slot_rel[ip * 2 + 1] = (sems["DV"], dv_p)
                    kpos = grp.index(it)
                    if kpos < 3:
                        emit_down_fills(gi, grp, kpos)
                mst["ac_floor"] = cnt["AC"]
                slot_seen = set()
                for qtr in range(8):
                    for hml in range(4):
                        hm = qtr * 4 + hml
                        is_fg = final and gi == len(GRPS) - 1
                        nslot = 5 if gi < len(GRPS) - 1 else 4
                        slot_of = {0: rot[0] % nslot, 1: (rot[0] + 1) % nslot}
                        rot[0] += 2
                        pe_s2 = [0, 0]
                        first = True
                        for ktl in range(Gn):
                            bix = dbuf[(gi, qtr, ktl)]
                            for s2 in range(2):
                                slot = slot_of[s2]
                                waits = []
                                if ktl == 0:
                                    waits.append(psd4_free[slot])
                                    if slot not in slot_seen:
                                        slot_seen.add(slot)
                                        waits.append(guard_of.get(slot))
                                    if first and hml == 0:
                                        for k2 in range(Gn):
                                            waits.append(dfin[(gi, qtr, k2)])
                                if qtr == 0 and hml == 0 and s2 == 0:
                                    waits.append((sems["DV"], pTT_by_ktl[ktl]))

                                first = False
                                pe = T(lambda t, _k=ktl, _s2=s2, _slot=slot,
                                       _hml=hml, _b=bix, _G=Gn: t.matmul(
                                           psd4[_slot],
                                           dstg[:, _b, _hml, :],
                                           pT[:, _k, _s2 * 512:(_s2 + 1) * 512],
                                           start=(_k == 0), stop=(_k == _G - 1)),
                                       waits=waits, inc=(ktl == Gn - 1))
                                if ktl == Gn - 1:
                                    pe_s2[s2] = pe
                        if hml == 3:
                            for ktl in range(Gn):
                                drel[dbuf[(gi, qtr, ktl)]] = (sems["PE"], pe_s2[1])
                        for s2 in range(2):
                            slot = slot_of[s2]
                            sl = slice(s2 * 512, (s2 + 1) * 512)
                            if is_fg:
                                dv_e = V(lambda v, _hm=hm, _slot=slot, _sl=sl:
                                         v.scalar_tensor_tensor(
                                             out_sb[:, _hm % 2, _sl], psd4[_slot],
                                             hds[:, _hm:_hm + 1], xT[:, _hm, _sl],
                                             op0=ALU.mult, op1=ALU.add),
                                         waits=[(sems["PE"], pe_s2[s2]),
                                                out_rel.get(hm % 2) if s2 == 0 else None])
                                psd4_free[slot] = (sems["DV"], dv_e)
                                if s2 == 1:
                                    out_rel[hm % 2] = DMA(
                                        "y", yt[hm], out_sb[:, hm % 2],
                                        f"O{hm % 2}", waits=[(sems["DV"], dv_e)])
                                    tile_fin[hm] = (sems["DV"], dv_e)
                            else:
                                dv_a = V(lambda v, _hm=hm, _slot=slot, _sl=sl:
                                         v.scalar_tensor_tensor(
                                             xT[:, _hm, _sl], psd4[_slot],
                                             hds[:, _hm:_hm + 1],
                                             xT[:, _hm, _sl],
                                             op0=ALU.mult, op1=ALU.add),
                                         waits=[(sems["PE"], pe_s2[s2])])
                                psd4_free[slot] = (sems["DV"], dv_a)
                                if s2 == 1:
                                    tile_fin[hm] = (sems["DV"], dv_a)
                        if pass_idx == 0 and (hm % 4) == 3:
                            if mst["li"] < 2:
                                emit_mask_load()
                                emit_mask_load()
                            else:
                                emit_mask_comp()
                                emit_mask_load()
                    if qtr < 5:
                        emit_down_fills(gi, grp, qtr + 3)
                    del qtr
                def _mx(a, b):
                    if a is None:
                        return b
                    if b is None:
                        return a
                    return a if a[1] >= b[1] else b
                slot_rel[0] = _mx(psd4_free[0], psd4_free[1])
                slot_rel[1] = _mx(psd4_free[2], psd4_free[3])
                slot_rel[2] = _mx(slot_rel[2], psd4_free[4])
                psd_free[0] = slot_rel[0]
                psd_free[1] = slot_rel[1]

        def emit_attn(ln_idx):
            w_all_wm0 = (sems["WMW0"], 16 * fills["WMW0"])
            w_all_wm1 = (sems["WMW1"], 16 * fills["WMW1"])
            ws_rel = {}
            slab = shp[:, 0:4096].rearrange("p (b kt j) -> p b kt j", b=2, j=128)
            for nt in range(HT):
                par = nt % 2
                src_nt = wmscr.rearrange("kt p (nt j) -> nt p kt j", j=128)[nt]
                w_s0 = DMA("y", slab[:, 0], src_nt[:, 0:16],
                           f"WS0",
                           waits=[w_all_wm0 if nt == 0 else None,
                                  w_all_wm1 if nt == 0 else None,
                                  ws_rel.get(0)])
                w_s1 = DMA("y", slab[:, 1], src_nt[:, 16:32], f"WS1",
                           waits=[ws_rel.get(1)])
                waits = [w_s0, psd_free[par]]
                if nt != 0:
                    waits.append((sems["DV"], norm_done[ln_idx]))
                pe_a = 0
                for kt in range(HT):
                    if kt == 16:
                        waits.append(w_s1)
                    if nt == 0:
                        waits.append((sems["DV"], norm_hc[ln_idx][kt]))
                    for s2 in range(2):
                        pe_a = T(lambda t, _kt=kt, _s2=s2, _par=par:
                                 t.matmul(
                                     psd[_par][:, _s2 * 512:(_s2 + 1) * 512],
                                     slab[:, _kt // 16, _kt % 16, :],
                                     lnb[:, _kt, _s2 * 512:(_s2 + 1) * 512],
                                     start=(_kt == 0), stop=(_kt == HT - 1)),
                                 waits=waits,
                                 inc=((kt == 15 or kt == HT - 1) and s2 == 1))
                        waits = []
                        if kt == 15 and s2 == 1:
                            ws_rel[0] = (sems["PE"], cnt["PE"])
                ws_rel[1] = (sems["PE"], pe_a)
                dv_e = V(lambda v, _nt=nt, _par=par: v.scalar_tensor_tensor(
                    xT[:, _nt, :], psd[_par], bi_s[:, _nt:_nt + 1],
                    xT[:, _nt, :], op0=ALU.add, op1=ALU.add),
                    waits=[(sems["PE"], pe_a)])
                tile_fin[nt] = (sems["DV"], dv_e)
                psd_free[par] = (sems["DV"], dv_e)
            slot_rel[0] = psd_free[0]
            slot_rel[1] = psd_free[1]
            psd4_free[0] = psd4_free[1] = psd_free[0]
            psd4_free[2] = psd4_free[3] = psd_free[1]

        emit_ln(0, first=True)
        emit_ffn(0, 0)
        while mst["ci"] < len(mask_units):
            emit_mask_comp()
            emit_mask_load()
        emit_ln(1)
        emit_attn(1)
        emit_ln(2)
        emit_ffn(1, 2, final=True)
        YW([(sems["O0"], 16 * fills["O0"]), (sems["O1"], 16 * fills["O1"])])

        @block.tensor
        def _(t):
            for f in prog["t"]:
                f(t)

        @block.vector
        def _(v):
            for f in prog["v"]:
                f(v)

        @block.scalar
        def _(s):
            for f in prog["s"]:
                f(s)

        @block.gpsimd
        def _(g):
            for f in prog["g"]:
                f(g)

        @block.sync
        def _(y):
            for f in prog["y"]:
                f(y)

    return nc


_NC_CACHE = None


def _get_nc():
    global _NC_CACHE
    if _NC_CACHE is None:
        _NC_CACHE = build_nc()
    return _NC_CACHE


def kernel(L0_attn_data, L0_attn_mask, L0_attn_weight, L0_attn_bias,
           L0_ffn_gate_wq, L0_ffn_gate_scale, L0_ffn_up_wq, L0_ffn_up_scale,
           L0_ffn_down_wq, L0_ffn_down_scale):
    x = np.ascontiguousarray(np.asarray(L0_attn_data, dtype=np.float32)
                             ).reshape(8192, 4096)
    mask = np.ascontiguousarray(np.asarray(L0_attn_mask, dtype=np.int32))
    weight = np.ascontiguousarray(np.asarray(L0_attn_weight, dtype=np.float32))
    bias = np.asarray(L0_attn_bias, dtype=np.float32)
    gwq = np.asarray(L0_ffn_gate_wq, dtype=np.int32)
    uwq = np.asarray(L0_ffn_up_wq, dtype=np.int32)
    dwq = np.asarray(L0_ffn_down_wq, dtype=np.int32)
    gsc = np.asarray(L0_ffn_gate_scale, dtype=np.float32)
    usc = np.asarray(L0_ffn_up_scale, dtype=np.float32)
    dsc = np.asarray(L0_ffn_down_scale, dtype=np.float32)

    gw = gwq.reshape(IT, 128, HT, 128).transpose(0, 3, 2, 1)
    uw = uwq.reshape(IT, 128, HT, 128).transpose(0, 3, 2, 1)
    guw = np.ascontiguousarray(np.stack((gw, uw), axis=3).astype(np.int8))
    dw = np.ascontiguousarray(
        dwq.reshape(HT, 128, IT, 128).transpose(2, 3, 0, 1).astype(np.int8))
    aww = np.ascontiguousarray(weight.reshape(HT, 128, HT, 128).transpose(2, 3, 0, 1))
    mkk = np.ascontiguousarray(mask.T.astype(np.int8)).reshape(HT, 128, 4096)
    gs2 = np.ascontiguousarray(gsc.reshape(IT, 128).T)
    us2 = np.ascontiguousarray(usc.reshape(IT, 128).T)
    ds2 = np.ascontiguousarray(dsc.reshape(HT, 128).T)
    bi2 = np.ascontiguousarray(bias.reshape(HT, 128).T)
    onesb = np.ones((128, 1), ml_dtypes.bfloat16)
    onesr = np.ones((1, 128), ml_dtypes.bfloat16)
    a128 = (np.arange(128)[:, None] // 32 == np.arange(128)[None, :] // 32
            ).astype(np.float32)

    nc = _get_nc()
    in_maps = []
    for c in range(NCORE):
        xs = np.ascontiguousarray(x[c * S:(c + 1) * S, :].T).reshape(HT, 128, S)
        in_maps.append({"xin": xs, "guw": guw, "dw": dw, "aw": aww,
                        "mk": mkk, "gs": gs2, "us": us2, "ds": ds2, "bi": bi2,
                        "onesb": onesb, "onesr": onesr, "a128": a128})
    res = run_bass_kernel_spmd(nc, in_maps, core_ids=list(range(NCORE)))
    out = np.empty((8192, 4096), np.float32)
    for c in range(NCORE):
        ytc = np.asarray(res.results[c]["yt"])
        out[c * S:(c + 1) * S, :] = ytc.reshape(4096, S).T
    return out.reshape(1, 8192, 4096)


# version 19
# speedup vs baseline: 1.1003x; 1.1003x over previous
import numpy as np
import ml_dtypes
import concourse.bass as bass
import concourse.mybir as mybir
from concourse.bass_utils import run_bass_kernel_spmd

F32 = mybir.dt.float32
BF16 = mybir.dt.bfloat16
I32 = mybir.dt.int32
I8 = mybir.dt.int8
AF = mybir.ActivationFunctionType
ALU = mybir.AluOpType
AX = mybir.AxisListType

NCORE = 8
S = 1024
HT = 32
IT = 86
GRPS = []
_i = 0
for _g in [3] * 2 + [5] * 16:
    GRPS.append(list(range(_i, _i + _g)))
    _i += _g
assert _i == IT
GMAX = 5
DPOOL = 15
EPS = 1e-5


def build_nc():
    nc = bass.Bass("TRN2", target_bir_lowering=False, debug=False)

    xin = nc.dram_tensor("xin", [HT, 128, S], F32, kind="ExternalInput").ap()
    guw = nc.dram_tensor("guw", [IT, 128, HT, 2, 128], I8,
                         kind="ExternalInput").ap()
    dw = nc.dram_tensor("dw", [IT, 128, HT, 128], I8, kind="ExternalInput").ap()
    aw = nc.dram_tensor("aw", [HT, 128, HT, 128], F32, kind="ExternalInput").ap()
    mk = nc.dram_tensor("mk", [HT, 128, 4096], I8, kind="ExternalInput").ap()
    gs = nc.dram_tensor("gs", [128, IT], F32, kind="ExternalInput").ap()
    us = nc.dram_tensor("us", [128, IT], F32, kind="ExternalInput").ap()
    ds = nc.dram_tensor("ds", [128, HT], F32, kind="ExternalInput").ap()
    bi = nc.dram_tensor("bi", [128, HT], F32, kind="ExternalInput").ap()
    onesb_d = nc.dram_tensor("onesb", [128, 1], BF16, kind="ExternalInput").ap()
    onesr_d = nc.dram_tensor("onesr", [1, 128], BF16, kind="ExternalInput").ap()
    a128_d = nc.dram_tensor("a128", [128, 128], F32, kind="ExternalInput").ap()
    yt = nc.dram_tensor("yt", [HT, 128, S], F32, kind="ExternalOutput").ap()
    wmscr = nc.dram_tensor("wmscr", [HT, 128, 4096], BF16, kind="Internal").ap()

    from contextlib import ExitStack
    with ExitStack() as ctx:
        E = ctx.enter_context
        xT = E(nc.sbuf_tensor("xT", [128, HT, S], BF16))
        lnb = E(nc.sbuf_tensor("lnb", [128, HT, S], BF16))
        pT = E(nc.sbuf_tensor("pT", [128, GMAX, S], BF16))
        gustg = E(nc.sbuf_tensor("gustg", [128, 3, 8, 2, 128], BF16))
        dstg = E(nc.sbuf_tensor("dstg", [128, DPOOL, 4, 128], BF16))
        g_sb = E(nc.sbuf_tensor("g_sb", [128, S], BF16))
        tmp_n = E(nc.sbuf_tensor("tmp_n", [128, S], BF16))
        tmp_e = E(nc.sbuf_tensor("tmp_e", [128, 5, 512], BF16))
        out_sb = E(nc.sbuf_tensor("out_sb", [128, 2, S], F32))
        stats = E(nc.sbuf_tensor("stats", [1, 2048], BF16))
        shp = E(nc.sbuf_tensor("shp", [128, 4096], BF16))
        shpf = E(nc.sbuf_tensor("shpf", [128, 2048], F32))
        onesb = E(nc.sbuf_tensor("onesb_s", [128, 1], BF16))
        onesr = E(nc.sbuf_tensor("onesr_s", [1, 128], BF16))
        a128 = E(nc.sbuf_tensor("a128_s", [128, 128], F32))
        s1f = E(nc.sbuf_tensor("s1f", [128, 2, 32], F32))
        scmb = E(nc.sbuf_tensor("scmb", [128, IT], F32))
        hds = E(nc.sbuf_tensor("hds", [128, HT], F32))
        bi_s = E(nc.sbuf_tensor("bi_s", [128, HT], F32))
        m01 = E(nc.sbuf_tensor("m01", [128, 2, 32], BF16))
        pA = E(nc.psum_tensor("pA", [128, 2048], F32))
        pB = E(nc.psum_tensor("pB", [128, 2048], F32))

        sem_names = (["DV", "AC", "PE", "GP", "XIN", "XI1", "XI2", "XI3",
                      "CST", "WMW0", "WMW1",
                      "G0", "G1", "G2",
                      "MK0", "MK1", "AW0", "AW1",
                      "WS0", "WS1", "WH0", "WH1", "O0", "O1"] + [f"D{i}" for i in range(DPOOL)])
        sems = {n: E(nc.semaphore("s" + n)) for n in sem_names}
        block = E(nc.Block())

        prog = {k: [] for k in "tvsgy"}
        cnt = {"DV": 0, "AC": 0, "PE": 0, "GP": 0}
        fills = {n: 0 for n in sem_names}

        def V(build, waits=()):
            w = tuple([(sems["DV"], cnt["DV"])] +
                      [x for x in waits if x is not None])
            cnt["DV"] += 1

            def f(v, _w=w, _b=build):
                for sh, val in _w:
                    if val > 0:
                        v.wait_ge(sh, val)
                _b(v).then_inc(sems["DV"], 1)
            prog["v"].append(f)
            return cnt["DV"]

        def A(build, waits=()):
            w = tuple([(sems["AC"], cnt["AC"])] +
                      [x for x in waits if x is not None])
            cnt["AC"] += 1

            def f(s, _w=w, _b=build):
                for sh, val in _w:
                    if val > 0:
                        s.wait_ge(sh, val)
                _b(s).then_inc(sems["AC"], 1)
            prog["s"].append(f)
            return cnt["AC"]

        def G(build, waits=()):
            w = tuple([(sems["GP"], cnt["GP"])] +
                      [x for x in waits if x is not None])
            cnt["GP"] += 1

            def f(g, _w=w, _b=build):
                for sh, val in _w:
                    if val > 0:
                        g.wait_ge(sh, val)
                _b(g).then_inc(sems["GP"], 1)
            prog["g"].append(f)
            return cnt["GP"]

        def T(build, waits=(), inc=False):
            if inc:
                cnt["PE"] += 1

            def f(t, _w=tuple(x for x in waits if x is not None),
                  _b=build, _i=inc):
                for sh, val in _w:
                    if val > 0:
                        t.wait_ge(sh, val)
                op = _b(t)
                if _i:
                    op.then_inc(sems["PE"], 1)
            prog["t"].append(f)
            return cnt["PE"]

        def DMA(e, dst, src, sem_name, waits=()):
            fills[sem_name] += 1
            val = 16 * fills[sem_name]

            def f(eng, _w=tuple(x for x in waits if x is not None),
                  _d=dst, _s=src, _n=sem_name):
                for sh, v in _w:
                    if v > 0:
                        eng.wait_ge(sh, v)
                eng.dma_start(out=_d, in_=_s).then_inc(sems[_n], 16)
            prog[e].append(f)
            return (sems[sem_name], val)

        def YW(waits):
            def f(eng, _w=tuple(x for x in waits if x is not None)):
                for sh, v in _w:
                    if v > 0:
                        eng.wait_ge(sh, v)
            prog["y"].append(f)

        w_xchunk = []
        for xc, xsem in enumerate(("XIN", "XI1", "XI2", "XI3")):
            w_xchunk.append(DMA(
                "g", xT[:, 8 * xc:8 * (xc + 1), :],
                xin.rearrange("hc p s -> p hc s")[:, 8 * xc:8 * (xc + 1)], xsem))
        w_xin = w_xchunk[-1]
        gs_s = shpf[:, 0:86]
        us_s = shpf[:, 86:172]
        ds_s = shpf[:, 172:204]
        for dst, src in ((onesb[:], onesb_d[:, :]),
                         (onesr[:], onesr_d[:, :]),
                         (a128[:], a128_d[:, :]), (gs_s, gs[:, :]),
                         (us_s, us[:, :]), (ds_s, ds[:, :]),
                         (bi_s[:], bi[:, :])):
            DMA("y", dst, src, "CST")
        w_cst = (sems["CST"], 16 * fills["CST"])
        V(lambda v: v.tensor_tensor(scmb[:], gs_s, us_s, op=ALU.mult),
          waits=[w_cst])
        V(lambda v: v.tensor_scalar(hds[:], ds_s, 0.5, None, op0=ALU.mult))
        dv_consts = cnt["DV"]

        for wi_ in range(80):
            T(lambda t: t.matmul(pA[:, 0:128], a128[:, :], a128[:, :],
                                 start=True, stop=True),
              waits=[w_cst] if wi_ == 0 else ())

        psg = [pA[:, 0:1024], pB[:, 0:1024]]
        psu = [pA[:, 1024:2048], pB[:, 1024:2048]]
        psd = [pA[:, 0:1024], pA[:, 1024:2048]]
        psd4 = [pA[:, 0:512], pA[:, 512:1024], pA[:, 1024:1536],
                pA[:, 1536:2048], pB[:, 512:1024]]
        meanB = pB[:, 0:1024]
        rstdB = pB[:, 1024:2048]
        psSx = pB[0:1, 0:1024]
        psSq = pB[0:1, 1024:2048]
        psM = pB[:, 0:32]

        norm_done = {}
        norm_hc = {}
        gsb_rel = [None]
        tmpe_rel = {0: None, 1: None, 2: None, 3: None, 4: None}
        psd4_free = {0: None, 1: None, 2: None, 3: None, 4: None}
        tile_fin = {}
        psd_free = {0: None, 1: None}
        slot_rel = [None, None, None, None]
        gu_rel = {"G": {}, "U": {}}
        drel = {}

        mkbuf = shp[:, 0:2048].rearrange("p (b n) -> p b n", b=2)
        wmbuf = shp[:, 2048:4096].rearrange("p (b n) -> p b n", b=2)
        awbuf = shpf[:, 0:2048].rearrange("p (b n) -> p b n", b=2)
        mst = {"li": 0, "ci": 0, "mk_rel": {}, "aw_rel": {}, "wm_rel": {},
               "aw_fill": {}, "red_dv": {}, "last_sign_ac": 0, "a128_pe": 0,
               "ac_floor": 0}
        mask_units = [(kt, q) for kt in range(HT) for q in range(4)]
        MLAG = 2

        def emit_mask_load():
            u = mst["li"]
            if u >= len(mask_units):
                return
            mst["li"] += 1
            kt, q = mask_units[u]
            b = u % 2
            w_mk = DMA("g", mkbuf[:, b], mk[kt, :, 1024 * q:1024 * (q + 1)],
                       f"MK{b}",
                       waits=[mst["mk_rel"].get(b),
                              (sems["DV"], dv_consts) if u < 2 else None])
            mst["aw_fill"][u] = DMA(
                "y", awbuf[:, b],
                aw[kt, :, 8 * q:8 * q + 8, :].rearrange("p a b -> p (a b)"),
                f"AW{b}",
                waits=[mst["aw_rel"].get(b),
                       (sems["DV"], dv_consts) if u < 2 else None])
            dv_red = V(lambda v, _m=mkbuf[:, b], _b=b: v.tensor_reduce(
                s1f[:, _b], _m.rearrange("p (a c) -> p a c", c=32),
                axis=AX.X, op=ALU.add),
                waits=[w_mk, (sems["PE"], mst["a128_pe"])])
            mst["mk_rel"][b] = (sems["DV"], dv_red)
            mst["red_dv"][u] = dv_red

        def emit_mask_comp():
            u = mst["ci"]
            if u >= len(mask_units) or u >= mst["li"]:
                return
            mst["ci"] += 1
            kt, q = mask_units[u]
            b = u % 2
            pe_a = T(lambda t, _b=b: t.matmul(psM, a128[:, :], s1f[:, _b],
                                              start=True, stop=True),
                     waits=[(sems["DV"], mst["red_dv"][u]),
                            (sems["AC"], max(mst["last_sign_ac"],
                                             mst["ac_floor"])), w_cst],
                     inc=True)
            mst["a128_pe"] = pe_a
            ac_s = A(lambda s, _b=b: s.activation(m01[:, _b], psM, AF.Sign),
                     waits=[(sems["PE"], pe_a)])
            mst["last_sign_ac"] = ac_s
            m_b = m01[:, b].unsqueeze(2).broadcast_to([128, 32, 32])
            dv_wm = V(lambda v, _b=b, _mb=m_b: v.tensor_tensor(
                wmbuf[:, _b].rearrange("p (a c) -> p a c", c=32),
                awbuf[:, _b].rearrange("p (a c) -> p a c", c=32),
                _mb, op=ALU.mult),
                waits=[mst["aw_fill"][u], (sems["AC"], ac_s),
                       mst["wm_rel"].get(b)])
            mst["aw_rel"][b] = (sems["DV"], dv_wm)
            mst["wm_rel"][b] = DMA(
                "y", wmscr[kt, :, 1024 * q:1024 * (q + 1)], wmbuf[:, b],
                f"WMW{b}", waits=[(sems["DV"], dv_wm)])

        def emit_ln(ln_idx, first=False):
            sq_pe_prev = 0
            for hc in range(HT):
                waits = []
                if hc == 0:
                    waits.append((sems["PE"], mst["a128_pe"]))
                    waits.append((sems["AC"], mst["last_sign_ac"]))
                    if first:
                        waits.append(w_cst)
                if first:
                    if hc % 8 == 0:
                        waits.append(w_xchunk[hc // 8])
                elif hc in tile_fin:
                    waits.append(tile_fin[hc])
                T(lambda t, _hc=hc: t.matmul(
                    psSx[:, 0:512], onesb[:, :], xT[:, _hc, 0:512],
                    start=(_hc == 0), stop=(_hc == HT - 1)), waits=waits)
                T(lambda t, _hc=hc: t.matmul(
                    psSx[:, 512:1024], onesb[:, :], xT[:, _hc, 512:1024],
                    start=(_hc == 0), stop=(_hc == HT - 1)))
                sq_w = [(sems["PE"], sq_pe_prev)]
                if first:
                    sq_w.append(w_xchunk[hc // 8])
                elif hc in tile_fin:
                    sq_w.append(tile_fin[hc])
                ac_sq = A(lambda s, _hc=hc: s.activation(
                    tmp_n[:], xT[:, _hc, :], AF.Square), waits=sq_w)
                T(lambda t, _hc=hc: t.matmul(
                    psSq[:, 0:512], onesb[:, :], tmp_n[:, 0:512],
                    start=(_hc == 0), stop=(_hc == HT - 1)),
                  waits=[(sems["AC"], ac_sq)])
                sq_pe_prev = T(lambda t, _hc=hc: t.matmul(
                    psSq[:, 512:1024], onesb[:, :], tmp_n[:, 512:1024],
                    start=(_hc == 0), stop=(_hc == HT - 1)), inc=True)
            mean_v = stats[0:1, 0:1024]
            rstd_v = stats[0:1, 1024:2048]
            scr0 = out_sb[0:1, 0, :]
            scr1 = out_sb[0:1, 1, :]
            V(lambda v: v.tensor_scalar(mean_v, psSx, 1.0 / 4096, None,
                                        op0=ALU.mult),
              waits=[(sems["PE"], sq_pe_prev)])
            V(lambda v: v.tensor_scalar(scr0, psSq, 1.0 / 4096, None,
                                        op0=ALU.mult))
            V(lambda v: v.tensor_tensor(scr1, mean_v, mean_v, op=ALU.mult))
            V(lambda v: v.tensor_tensor(scr0, scr0, scr1, op=ALU.subtract))
            dv_var = V(lambda v: v.tensor_scalar(scr1, scr0, EPS, None,
                                                 op0=ALU.add))
            ac_std = A(lambda s: s.activation(scr0, scr1, AF.Sqrt),
                       waits=[(sems["DV"], dv_var)])
            V(lambda v: v.reciprocal(scr1, scr0), waits=[(sems["AC"], ac_std)])
            dv_rstd = V(lambda v: v.tensor_copy(rstd_v, scr1))
            pe_bc = 0
            for half in range(2):
                T(lambda t, _h=half: t.matmul(
                    meanB[:, _h * 512:(_h + 1) * 512], onesr[0:1, :],
                    mean_v[:, _h * 512:(_h + 1) * 512], start=True, stop=True),
                  waits=[(sems["DV"], dv_rstd)] if half == 0 else ())
                pe_bc = T(lambda t, _h=half: t.matmul(
                    rstdB[:, _h * 512:(_h + 1) * 512], onesr[0:1, :],
                    rstd_v[:, _h * 512:(_h + 1) * 512], start=True, stop=True),
                    inc=(half == 1))
            mb_sb = dstg[:, 11:13, :, :].rearrange("p b h j -> p (b h j)")
            rs_sb = dstg[:, 13:15, :, :].rearrange("p b h j -> p (b h j)")
            V(lambda v: v.tensor_copy(mb_sb, meanB), waits=[(sems["PE"], pe_bc)])
            V(lambda v: v.tensor_copy(rs_sb, rstdB))
            dvn = 0
            nhc = []
            for hc in range(HT):
                V(lambda v, _hc=hc: v.tensor_tensor(
                    tmp_n[:], xT[:, _hc, :], mb_sb, op=ALU.subtract))
                dvn = V(lambda v, _hc=hc: v.tensor_tensor(
                    lnb[:, _hc, :], tmp_n[:], rs_sb, op=ALU.mult))
                nhc.append(dvn)
            norm_done[ln_idx] = dvn
            norm_hc[ln_idx] = nhc
            tile_fin.clear()

        def emit_ffn(pass_idx, ln_idx, final=False):
            dseq = 0
            dbuf = {}
            dfin = {}
            pTT_by_ktl = {}
            pA_guard = [None, None]
            pB_guard = [None, None]
            guard_of = {}
            slot_seen = set()
            rot = [0]
            out_rel = {}
            pTT_last = 0
            next_ip = 0
            for gi, grp in enumerate(GRPS):
                Gn = len(grp)

                def emit_down_fills(_gi, _grp, _ck):
                    nonlocal dseq
                    for _ktl, _it in enumerate(_grp):
                        b = dseq % DPOOL
                        dbuf[(_gi, _ck, _ktl)] = b
                        w = [drel.get(b)]
                        if dseq < DPOOL and b >= 11:
                            w.append((sems["DV"], norm_done[ln_idx]))
                        dfin[(_gi, _ck, _ktl)] = DMA(
                            "g", dstg[:, b], dw[_it, :, 4 * _ck:4 * (_ck + 1), :],
                            f"D{b}", waits=w)
                        dseq += 1
                if gi > 0:
                    next_ip = 1
                for it in grp:
                    ip = next_ip
                    next_ip = 1 - next_ip
                    pe_g = pe_u = 0
                    chase = (gi == 0 and it == grp[0])
                    for h in range(4):
                        b = (it * 4 + h) % 3
                        w_fill = DMA(
                            "g", gustg[:, b],
                            guw[it, :, 8 * h:8 * (h + 1), :, :],
                            f"G{b}", waits=[gu_rel["G"].get(b)])
                        waits = [w_fill]
                        if h == 0:
                            waits.append(slot_rel[ip * 2])
                            waits.append(slot_rel[ip * 2 + 1])
                            if not chase:
                                waits.append((sems["DV"], norm_done[ln_idx]))
                            waits.append((sems["AC"], mst["last_sign_ac"]))
                        for wname, wi, ps in (("G", 0, psg[ip]),
                                              ("U", 1, psu[ip])):
                            for hcl in range(8):
                                hc = h * 8 + hcl
                                if chase and wname == "G":
                                    waits.append((sems["DV"], norm_hc[ln_idx][hc]))
                                for s2 in range(2):
                                    pe = T(lambda t, _b=b, _hcl=hcl, _wi=wi,
                                           _hc=hc, _s2=s2, _ps=ps: t.matmul(
                                               _ps[:, _s2 * 512:(_s2 + 1) * 512],
                                               gustg[:, _b, _hcl, _wi, :],
                                               lnb[:, _hc, _s2 * 512:(_s2 + 1) * 512],
                                               start=(_hc == 0),
                                               stop=(_hc == HT - 1)),
                                           waits=waits,
                                           inc=(hcl == 7 and s2 == 1))
                                    waits = []
                            if wname == "G":
                                pe_g = pe
                            else:
                                pe_u = pe
                                gu_rel["G"][b] = (sems["PE"], pe)
                    ac_g = A(lambda s, _it=it, _ip=ip: s.activation(
                        g_sb[:], psg[_ip], AF.Copy, scale=scmb[:, _it:_it + 1]),
                        waits=[(sems["PE"], pe_g), (sems["DV"], dv_consts),
                               gsb_rel[0]])
                    ktl = grp.index(it)
                    dv_p = V(lambda v, _ip=ip, _k=ktl: v.tensor_tensor(
                        pT[:, _k, :], g_sb[:], psu[_ip], op=ALU.mult),
                        waits=[(sems["AC"], ac_g), (sems["PE"], pe_u)])
                    pTT_last = dv_p
                    pTT_by_ktl[ktl] = dv_p
                    if ip == 0:
                        pA_guard[0] = (sems["AC"], ac_g)
                        pA_guard[1] = (sems["DV"], dv_p)
                    else:
                        pB_guard[0] = (sems["AC"], ac_g)
                        pB_guard[1] = (sems["DV"], dv_p)
                    guard_of = {0: pA_guard[0], 1: pA_guard[0],
                                2: pA_guard[1], 3: pA_guard[1],
                                4: pB_guard[0]}
                    gsb_rel[0] = (sems["DV"], dv_p)
                    slot_rel[ip * 2] = (sems["AC"], ac_g)
                    slot_rel[ip * 2 + 1] = (sems["DV"], dv_p)
                    kpos = grp.index(it)
                    if kpos < 3:
                        emit_down_fills(gi, grp, kpos)
                mst["ac_floor"] = cnt["AC"]
                slot_seen = set()
                for qtr in range(8):
                    for hml in range(4):
                        hm = qtr * 4 + hml
                        is_fg = final and gi == len(GRPS) - 1
                        nslot = 5 if gi < len(GRPS) - 1 else 4
                        slot_of = {0: rot[0] % nslot, 1: (rot[0] + 1) % nslot}
                        rot[0] += 2
                        pe_s2 = [0, 0]
                        first = True
                        for ktl in range(Gn):
                            bix = dbuf[(gi, qtr, ktl)]
                            for s2 in range(2):
                                slot = slot_of[s2]
                                waits = []
                                if ktl == 0:
                                    waits.append(psd4_free[slot])
                                    if slot not in slot_seen:
                                        slot_seen.add(slot)
                                        waits.append(guard_of.get(slot))
                                    if first and hml == 0:
                                        for k2 in range(Gn):
                                            waits.append(dfin[(gi, qtr, k2)])
                                if qtr == 0 and hml == 0 and s2 == 0:
                                    waits.append((sems["DV"], pTT_by_ktl[ktl]))

                                first = False
                                pe = T(lambda t, _k=ktl, _s2=s2, _slot=slot,
                                       _hml=hml, _b=bix, _G=Gn: t.matmul(
                                           psd4[_slot],
                                           dstg[:, _b, _hml, :],
                                           pT[:, _k, _s2 * 512:(_s2 + 1) * 512],
                                           start=(_k == 0), stop=(_k == _G - 1)),
                                       waits=waits, inc=(ktl == Gn - 1))
                                if ktl == Gn - 1:
                                    pe_s2[s2] = pe
                        if hml == 3:
                            for ktl in range(Gn):
                                drel[dbuf[(gi, qtr, ktl)]] = (sems["PE"], pe_s2[1])
                        for s2 in range(2):
                            slot = slot_of[s2]
                            sl = slice(s2 * 512, (s2 + 1) * 512)
                            if is_fg:
                                dv_e = V(lambda v, _hm=hm, _slot=slot, _sl=sl:
                                         v.scalar_tensor_tensor(
                                             out_sb[:, _hm % 2, _sl], psd4[_slot],
                                             hds[:, _hm:_hm + 1], xT[:, _hm, _sl],
                                             op0=ALU.mult, op1=ALU.add),
                                         waits=[(sems["PE"], pe_s2[s2]),
                                                out_rel.get(hm % 2) if s2 == 0 else None])
                                psd4_free[slot] = (sems["DV"], dv_e)
                                if s2 == 1:
                                    out_rel[hm % 2] = DMA(
                                        "y", yt[hm], out_sb[:, hm % 2],
                                        f"O{hm % 2}", waits=[(sems["DV"], dv_e)])
                                    tile_fin[hm] = (sems["DV"], dv_e)
                            else:
                                dv_a = V(lambda v, _hm=hm, _slot=slot, _sl=sl:
                                         v.scalar_tensor_tensor(
                                             xT[:, _hm, _sl], psd4[_slot],
                                             hds[:, _hm:_hm + 1],
                                             xT[:, _hm, _sl],
                                             op0=ALU.mult, op1=ALU.add),
                                         waits=[(sems["PE"], pe_s2[s2])])
                                psd4_free[slot] = (sems["DV"], dv_a)
                                if s2 == 1:
                                    tile_fin[hm] = (sems["DV"], dv_a)
                        if pass_idx == 0 and (hm % 4) == 3:
                            if mst["li"] < 2:
                                emit_mask_load()
                                emit_mask_load()
                            else:
                                emit_mask_comp()
                                emit_mask_load()
                    if qtr < 5:
                        emit_down_fills(gi, grp, qtr + 3)
                    del qtr
                def _mx(a, b):
                    if a is None:
                        return b
                    if b is None:
                        return a
                    return a if a[1] >= b[1] else b
                slot_rel[0] = _mx(psd4_free[0], psd4_free[1])
                slot_rel[1] = _mx(psd4_free[2], psd4_free[3])
                slot_rel[2] = _mx(slot_rel[2], psd4_free[4])
                psd_free[0] = slot_rel[0]
                psd_free[1] = slot_rel[1]

        def emit_attn(ln_idx):
            w_all_wm0 = (sems["WMW0"], 16 * fills["WMW0"])
            w_all_wm1 = (sems["WMW1"], 16 * fills["WMW1"])
            ws_rel = {}
            slab = shp[:, 0:4096].rearrange("p (b kt j) -> p b kt j", b=2, j=128)
            for nt in range(HT):
                par = nt % 2
                src_nt = wmscr.rearrange("kt p (nt j) -> nt p kt j", j=128)[nt]
                w_s0 = DMA("y", slab[:, 0], src_nt[:, 0:16],
                           f"WS0",
                           waits=[w_all_wm0 if nt == 0 else None,
                                  w_all_wm1 if nt == 0 else None,
                                  ws_rel.get(0)])
                w_s1 = DMA("y", slab[:, 1], src_nt[:, 16:32], f"WS1",
                           waits=[ws_rel.get(1)])
                waits = [w_s0, psd_free[par]]
                if nt != 0:
                    waits.append((sems["DV"], norm_done[ln_idx]))
                pe_a = 0
                for kt in range(HT):
                    if kt == 16:
                        waits.append(w_s1)
                    if nt == 0:
                        waits.append((sems["DV"], norm_hc[ln_idx][kt]))
                    for s2 in range(2):
                        pe_a = T(lambda t, _kt=kt, _s2=s2, _par=par:
                                 t.matmul(
                                     psd[_par][:, _s2 * 512:(_s2 + 1) * 512],
                                     slab[:, _kt // 16, _kt % 16, :],
                                     lnb[:, _kt, _s2 * 512:(_s2 + 1) * 512],
                                     start=(_kt == 0), stop=(_kt == HT - 1)),
                                 waits=waits,
                                 inc=((kt == 15 or kt == HT - 1) and s2 == 1))
                        waits = []
                        if kt == 15 and s2 == 1:
                            ws_rel[0] = (sems["PE"], cnt["PE"])
                ws_rel[1] = (sems["PE"], pe_a)
                dv_e = V(lambda v, _nt=nt, _par=par: v.scalar_tensor_tensor(
                    xT[:, _nt, :], psd[_par], bi_s[:, _nt:_nt + 1],
                    xT[:, _nt, :], op0=ALU.add, op1=ALU.add),
                    waits=[(sems["PE"], pe_a)])
                tile_fin[nt] = (sems["DV"], dv_e)
                psd_free[par] = (sems["DV"], dv_e)
            slot_rel[0] = psd_free[0]
            slot_rel[1] = psd_free[1]
            psd4_free[0] = psd4_free[1] = psd_free[0]
            psd4_free[2] = psd4_free[3] = psd_free[1]

        emit_ln(0, first=True)
        emit_ffn(0, 0)
        while mst["ci"] < len(mask_units):
            emit_mask_comp()
            emit_mask_load()
        emit_ln(1)
        emit_attn(1)
        emit_ln(2)
        emit_ffn(1, 2, final=True)
        YW([(sems["O0"], 16 * fills["O0"]), (sems["O1"], 16 * fills["O1"])])

        @block.tensor
        def _(t):
            for f in prog["t"]:
                f(t)

        @block.vector
        def _(v):
            for f in prog["v"]:
                f(v)

        @block.scalar
        def _(s):
            for f in prog["s"]:
                f(s)

        @block.gpsimd
        def _(g):
            for f in prog["g"]:
                f(g)

        @block.sync
        def _(y):
            for f in prog["y"]:
                f(y)

    return nc


_NC_CACHE = None


def _get_nc():
    global _NC_CACHE
    if _NC_CACHE is None:
        _NC_CACHE = build_nc()
    return _NC_CACHE


def kernel(L0_attn_data, L0_attn_mask, L0_attn_weight, L0_attn_bias,
           L0_ffn_gate_wq, L0_ffn_gate_scale, L0_ffn_up_wq, L0_ffn_up_scale,
           L0_ffn_down_wq, L0_ffn_down_scale):
    x = np.ascontiguousarray(np.asarray(L0_attn_data, dtype=np.float32)
                             ).reshape(8192, 4096)
    mask = np.ascontiguousarray(np.asarray(L0_attn_mask, dtype=np.int32))
    weight = np.ascontiguousarray(np.asarray(L0_attn_weight, dtype=np.float32))
    bias = np.asarray(L0_attn_bias, dtype=np.float32)
    gwq = np.asarray(L0_ffn_gate_wq, dtype=np.int32)
    uwq = np.asarray(L0_ffn_up_wq, dtype=np.int32)
    dwq = np.asarray(L0_ffn_down_wq, dtype=np.int32)
    gsc = np.asarray(L0_ffn_gate_scale, dtype=np.float32)
    usc = np.asarray(L0_ffn_up_scale, dtype=np.float32)
    dsc = np.asarray(L0_ffn_down_scale, dtype=np.float32)

    gw = gwq.reshape(IT, 128, HT, 128).transpose(0, 3, 2, 1)
    uw = uwq.reshape(IT, 128, HT, 128).transpose(0, 3, 2, 1)
    guw = np.ascontiguousarray(np.stack((gw, uw), axis=3).astype(np.int8))
    dw = np.ascontiguousarray(
        dwq.reshape(HT, 128, IT, 128).transpose(2, 3, 0, 1).astype(np.int8))
    aww = np.ascontiguousarray(weight.reshape(HT, 128, HT, 128).transpose(2, 3, 0, 1))
    mkk = np.ascontiguousarray(mask.T.astype(np.int8)).reshape(HT, 128, 4096)
    gs2 = np.ascontiguousarray(gsc.reshape(IT, 128).T)
    us2 = np.ascontiguousarray(usc.reshape(IT, 128).T)
    ds2 = np.ascontiguousarray(dsc.reshape(HT, 128).T)
    bi2 = np.ascontiguousarray(bias.reshape(HT, 128).T)
    onesb = np.ones((128, 1), ml_dtypes.bfloat16)
    onesr = np.ones((1, 128), ml_dtypes.bfloat16)
    a128 = (np.arange(128)[:, None] // 32 == np.arange(128)[None, :] // 32
            ).astype(np.float32)

    nc = _get_nc()
    in_maps = []
    for c in range(NCORE):
        xs = np.ascontiguousarray(x[c * S:(c + 1) * S, :].T).reshape(HT, 128, S)
        in_maps.append({"xin": xs, "guw": guw, "dw": dw, "aw": aww,
                        "mk": mkk, "gs": gs2, "us": us2, "ds": ds2, "bi": bi2,
                        "onesb": onesb, "onesr": onesr, "a128": a128})
    res = run_bass_kernel_spmd(nc, in_maps, core_ids=list(range(NCORE)))
    out = np.empty((8192, 4096), np.float32)
    for c in range(NCORE):
        ytc = np.asarray(res.results[c]["yt"])
        out[c * S:(c + 1) * S, :] = ytc.reshape(4096, S).T
    return out.reshape(1, 8192, 4096)


# version 21 (final)
# speedup vs baseline: 1.2018x; 1.0923x over previous
import numpy as np
import ml_dtypes
import concourse.bass as bass
import concourse.mybir as mybir
from concourse.bass_utils import run_bass_kernel_spmd

F32 = mybir.dt.float32
BF16 = mybir.dt.bfloat16
I32 = mybir.dt.int32
I8 = mybir.dt.int8
AF = mybir.ActivationFunctionType
ALU = mybir.AluOpType
AX = mybir.AxisListType

NCORE = 8
S = 1024
HT = 32
IT = 86
GRPS = []
_i = 0
for _g in [3] * 2 + [5] * 16:
    GRPS.append(list(range(_i, _i + _g)))
    _i += _g
assert _i == IT
GMAX = 5
DPOOL = 15
EPS = 1e-5


def build_nc():
    nc = bass.Bass("TRN2", target_bir_lowering=False, debug=False)

    xin = nc.dram_tensor("xin", [HT, 128, S], F32, kind="ExternalInput").ap()
    guw = nc.dram_tensor("guw", [IT, 128, HT, 2, 128], I8,
                         kind="ExternalInput").ap()
    dw = nc.dram_tensor("dw", [IT, 128, HT, 128], I8, kind="ExternalInput").ap()
    aw = nc.dram_tensor("aw", [HT, 128, HT, 128], F32, kind="ExternalInput").ap()
    mk = nc.dram_tensor("mk", [HT, 128, 4096], I8, kind="ExternalInput").ap()
    gs = nc.dram_tensor("gs", [128, IT], F32, kind="ExternalInput").ap()
    us = nc.dram_tensor("us", [128, IT], F32, kind="ExternalInput").ap()
    ds = nc.dram_tensor("ds", [128, HT], F32, kind="ExternalInput").ap()
    bi = nc.dram_tensor("bi", [128, HT], F32, kind="ExternalInput").ap()
    onesb_d = nc.dram_tensor("onesb", [128, 1], BF16, kind="ExternalInput").ap()
    onesr_d = nc.dram_tensor("onesr", [1, 128], BF16, kind="ExternalInput").ap()
    a128_d = nc.dram_tensor("a128", [128, 128], F32, kind="ExternalInput").ap()
    yt = nc.dram_tensor("yt", [HT, 128, S], F32, kind="ExternalOutput").ap()
    wmscr = nc.dram_tensor("wmscr", [HT, 128, 4096], BF16, kind="Internal").ap()

    from contextlib import ExitStack
    with ExitStack() as ctx:
        E = ctx.enter_context
        xT = E(nc.sbuf_tensor("xT", [128, HT, S], BF16))
        lnb = E(nc.sbuf_tensor("lnb", [128, HT, S], BF16))
        pT = E(nc.sbuf_tensor("pT", [128, GMAX, S], BF16))
        gustg = E(nc.sbuf_tensor("gustg", [128, 3, 8, 2, 128], BF16))
        dstg = E(nc.sbuf_tensor("dstg", [128, DPOOL, 4, 128], BF16))
        g_sb = E(nc.sbuf_tensor("g_sb", [128, S], BF16))
        tmp_n = E(nc.sbuf_tensor("tmp_n", [128, S], BF16))
        tmp_e = E(nc.sbuf_tensor("tmp_e", [128, 5, 512], BF16))
        out_sb = E(nc.sbuf_tensor("out_sb", [128, 2, S], F32))
        stats = E(nc.sbuf_tensor("stats", [1, 2048], BF16))
        shp = E(nc.sbuf_tensor("shp", [128, 4096], BF16))
        shpf = E(nc.sbuf_tensor("shpf", [128, 2048], F32))
        onesb = E(nc.sbuf_tensor("onesb_s", [128, 1], BF16))
        onesr = E(nc.sbuf_tensor("onesr_s", [1, 128], BF16))
        a128 = E(nc.sbuf_tensor("a128_s", [128, 128], F32))
        s1f = E(nc.sbuf_tensor("s1f", [128, 2, 32], F32))
        scmb = E(nc.sbuf_tensor("scmb", [128, IT], F32))
        hds = E(nc.sbuf_tensor("hds", [128, HT], F32))
        bi_s = E(nc.sbuf_tensor("bi_s", [128, HT], F32))
        m01 = E(nc.sbuf_tensor("m01", [128, 2, 32], BF16))
        pA = E(nc.psum_tensor("pA", [128, 2048], F32))
        pB = E(nc.psum_tensor("pB", [128, 2048], F32))

        sem_names = (["DV", "AC", "PE", "GP", "XIN", "XI1", "XI2", "XI3",
                      "CST", "WMW0", "WMW1",
                      "G0", "G1", "G2",
                      "MK0", "MK1", "AW0", "AW1",
                      "WS0", "WS1", "WH0", "WH1", "O0", "O1"] + [f"D{i}" for i in range(DPOOL)])
        sems = {n: E(nc.semaphore("s" + n)) for n in sem_names}
        block = E(nc.Block())

        prog = {k: [] for k in "tvsgy"}
        cnt = {"DV": 0, "AC": 0, "PE": 0, "GP": 0}
        fills = {n: 0 for n in sem_names}

        def V(build, waits=()):
            w = tuple([(sems["DV"], cnt["DV"])] +
                      [x for x in waits if x is not None])
            cnt["DV"] += 1

            def f(v, _w=w, _b=build):
                for sh, val in _w:
                    if val > 0:
                        v.wait_ge(sh, val)
                _b(v).then_inc(sems["DV"], 1)
            prog["v"].append(f)
            return cnt["DV"]

        def A(build, waits=()):
            w = tuple([(sems["AC"], cnt["AC"])] +
                      [x for x in waits if x is not None])
            cnt["AC"] += 1

            def f(s, _w=w, _b=build):
                for sh, val in _w:
                    if val > 0:
                        s.wait_ge(sh, val)
                _b(s).then_inc(sems["AC"], 1)
            prog["s"].append(f)
            return cnt["AC"]

        def G(build, waits=()):
            w = tuple([(sems["GP"], cnt["GP"])] +
                      [x for x in waits if x is not None])
            cnt["GP"] += 1

            def f(g, _w=w, _b=build):
                for sh, val in _w:
                    if val > 0:
                        g.wait_ge(sh, val)
                _b(g).then_inc(sems["GP"], 1)
            prog["g"].append(f)
            return cnt["GP"]

        def T(build, waits=(), inc=False):
            if inc:
                cnt["PE"] += 1

            def f(t, _w=tuple(x for x in waits if x is not None),
                  _b=build, _i=inc):
                for sh, val in _w:
                    if val > 0:
                        t.wait_ge(sh, val)
                op = _b(t)
                if _i:
                    op.then_inc(sems["PE"], 1)
            prog["t"].append(f)
            return cnt["PE"]

        def DMA(e, dst, src, sem_name, waits=()):
            fills[sem_name] += 1
            val = 16 * fills[sem_name]

            def f(eng, _w=tuple(x for x in waits if x is not None),
                  _d=dst, _s=src, _n=sem_name):
                for sh, v in _w:
                    if v > 0:
                        eng.wait_ge(sh, v)
                eng.dma_start(out=_d, in_=_s).then_inc(sems[_n], 16)
            prog[e].append(f)
            return (sems[sem_name], val)

        def YW(waits):
            def f(eng, _w=tuple(x for x in waits if x is not None)):
                for sh, v in _w:
                    if v > 0:
                        eng.wait_ge(sh, v)
            prog["y"].append(f)

        w_xchunk = []
        for xc, xsem in enumerate(("XIN", "XI1", "XI2", "XI3")):
            w_xchunk.append(DMA(
                "g", xT[:, 8 * xc:8 * (xc + 1), :],
                xin.rearrange("hc p s -> p hc s")[:, 8 * xc:8 * (xc + 1)], xsem))
        w_xin = w_xchunk[-1]
        gs_s = shpf[:, 0:86]
        us_s = shpf[:, 86:172]
        ds_s = shpf[:, 172:204]
        for dst, src in ((onesb[:], onesb_d[:, :]),
                         (onesr[:], onesr_d[:, :]),
                         (a128[:], a128_d[:, :]), (gs_s, gs[:, :]),
                         (us_s, us[:, :]), (ds_s, ds[:, :]),
                         (bi_s[:], bi[:, :])):
            DMA("y", dst, src, "CST")
        w_cst = (sems["CST"], 16 * fills["CST"])
        V(lambda v: v.tensor_tensor(scmb[:], gs_s, us_s, op=ALU.mult),
          waits=[w_cst])
        V(lambda v: v.tensor_scalar(hds[:], ds_s, 0.5, None, op0=ALU.mult))
        dv_consts = cnt["DV"]

        for wi_ in range(80):
            T(lambda t: t.matmul(pA[:, 0:128], a128[:, :], a128[:, :],
                                 start=True, stop=True),
              waits=[w_cst] if wi_ == 0 else ())

        psg = [pA[:, 0:1024], pB[:, 0:1024]]
        psu = [pA[:, 1024:2048], pB[:, 1024:2048]]
        psd = [pA[:, 0:1024], pA[:, 1024:2048]]
        psd4 = [pA[:, 0:512], pA[:, 512:1024], pA[:, 1024:1536],
                pA[:, 1536:2048], pB[:, 512:1024]]
        meanB = pB[:, 0:1024]
        rstdB = pB[:, 1024:2048]
        psSx = pB[0:1, 0:1024]
        psSq = pB[0:1, 1024:2048]
        psM = pB[:, 0:32]

        norm_done = {}
        norm_hc = {}
        gsb_rel = [None]
        tmpe_rel = {0: None, 1: None, 2: None, 3: None, 4: None}
        psd4_free = {0: None, 1: None, 2: None, 3: None, 4: None}
        tile_fin = {}
        psd_free = {0: None, 1: None}
        slot_rel = [None, None, None, None]
        gu_rel = {"G": {}, "U": {}}
        drel = {}

        mkbuf = shp[:, 0:2048].rearrange("p (b n) -> p b n", b=2)
        wmbuf = shp[:, 2048:4096].rearrange("p (b n) -> p b n", b=2)
        awbuf = shpf[:, 0:2048].rearrange("p (b n) -> p b n", b=2)
        mst = {"li": 0, "ci": 0, "mk_rel": {}, "aw_rel": {}, "wm_rel": {},
               "aw_fill": {}, "red_dv": {}, "last_sign_ac": 0, "a128_pe": 0,
               "ac_floor": 0}
        mask_units = [(kt, q) for kt in range(HT) for q in range(4)]
        MLAG = 2

        def emit_mask_load():
            u = mst["li"]
            if u >= len(mask_units):
                return
            mst["li"] += 1
            kt, q = mask_units[u]
            b = u % 2
            w_mk = DMA("g", mkbuf[:, b], mk[kt, :, 1024 * q:1024 * (q + 1)],
                       f"MK{b}",
                       waits=[mst["mk_rel"].get(b),
                              (sems["DV"], dv_consts) if u < 2 else None])
            mst["aw_fill"][u] = DMA(
                "y", awbuf[:, b],
                aw[kt, :, 8 * q:8 * q + 8, :].rearrange("p a b -> p (a b)"),
                f"AW{b}",
                waits=[mst["aw_rel"].get(b),
                       (sems["DV"], dv_consts) if u < 2 else None])
            dv_red = V(lambda v, _m=mkbuf[:, b], _b=b: v.tensor_reduce(
                s1f[:, _b], _m.rearrange("p (a c) -> p a c", c=32),
                axis=AX.X, op=ALU.add),
                waits=[w_mk, (sems["PE"], mst["a128_pe"])])
            mst["mk_rel"][b] = (sems["DV"], dv_red)
            mst["red_dv"][u] = dv_red

        def emit_mask_comp():
            u = mst["ci"]
            if u >= len(mask_units) or u >= mst["li"]:
                return
            mst["ci"] += 1
            kt, q = mask_units[u]
            b = u % 2
            pe_a = T(lambda t, _b=b: t.matmul(psM, a128[:, :], s1f[:, _b],
                                              start=True, stop=True),
                     waits=[(sems["DV"], mst["red_dv"][u]),
                            (sems["AC"], max(mst["last_sign_ac"],
                                             mst["ac_floor"])), w_cst],
                     inc=True)
            mst["a128_pe"] = pe_a
            ac_s = A(lambda s, _b=b: s.activation(m01[:, _b], psM, AF.Sign),
                     waits=[(sems["PE"], pe_a)])
            mst["last_sign_ac"] = ac_s
            m_b = m01[:, b].unsqueeze(2).broadcast_to([128, 32, 32])
            dv_wm = V(lambda v, _b=b, _mb=m_b: v.tensor_tensor(
                wmbuf[:, _b].rearrange("p (a c) -> p a c", c=32),
                awbuf[:, _b].rearrange("p (a c) -> p a c", c=32),
                _mb, op=ALU.mult),
                waits=[mst["aw_fill"][u], (sems["AC"], ac_s),
                       mst["wm_rel"].get(b)])
            mst["aw_rel"][b] = (sems["DV"], dv_wm)
            mst["wm_rel"][b] = DMA(
                "y", wmscr[kt, :, 1024 * q:1024 * (q + 1)], wmbuf[:, b],
                f"WMW{b}", waits=[(sems["DV"], dv_wm)])

        def emit_ln(ln_idx, first=False):
            sq_pe_prev = 0
            for hc in range(HT):
                waits = []
                if hc == 0:
                    waits.append((sems["PE"], mst["a128_pe"]))
                    waits.append((sems["AC"], mst["last_sign_ac"]))
                    if first:
                        waits.append(w_cst)
                if first:
                    if hc % 8 == 0:
                        waits.append(w_xchunk[hc // 8])
                elif hc in tile_fin:
                    waits.append(tile_fin[hc])
                T(lambda t, _hc=hc: t.matmul(
                    psSx[:, 0:512], onesb[:, :], xT[:, _hc, 0:512],
                    start=(_hc == 0), stop=(_hc == HT - 1)), waits=waits)
                T(lambda t, _hc=hc: t.matmul(
                    psSx[:, 512:1024], onesb[:, :], xT[:, _hc, 512:1024],
                    start=(_hc == 0), stop=(_hc == HT - 1)))
                sq_w = [(sems["PE"], sq_pe_prev)]
                if first:
                    sq_w.append(w_xchunk[hc // 8])
                elif hc in tile_fin:
                    sq_w.append(tile_fin[hc])
                ac_sq = A(lambda s, _hc=hc: s.activation(
                    tmp_n[:], xT[:, _hc, :], AF.Square), waits=sq_w)
                T(lambda t, _hc=hc: t.matmul(
                    psSq[:, 0:512], onesb[:, :], tmp_n[:, 0:512],
                    start=(_hc == 0), stop=(_hc == HT - 1)),
                  waits=[(sems["AC"], ac_sq)])
                sq_pe_prev = T(lambda t, _hc=hc: t.matmul(
                    psSq[:, 512:1024], onesb[:, :], tmp_n[:, 512:1024],
                    start=(_hc == 0), stop=(_hc == HT - 1)), inc=True)
            mean_v = stats[0:1, 0:1024]
            rstd_v = stats[0:1, 1024:2048]
            scr0 = out_sb[0:1, 0, :]
            scr1 = out_sb[0:1, 1, :]
            V(lambda v: v.tensor_scalar(mean_v, psSx, 1.0 / 4096, None,
                                        op0=ALU.mult),
              waits=[(sems["PE"], sq_pe_prev)])
            V(lambda v: v.tensor_scalar(scr0, psSq, 1.0 / 4096, None,
                                        op0=ALU.mult))
            V(lambda v: v.tensor_tensor(scr1, mean_v, mean_v, op=ALU.mult))
            V(lambda v: v.tensor_tensor(scr0, scr0, scr1, op=ALU.subtract))
            dv_var = V(lambda v: v.tensor_scalar(scr1, scr0, EPS, None,
                                                 op0=ALU.add))
            ac_std = A(lambda s: s.activation(scr0, scr1, AF.Sqrt),
                       waits=[(sems["DV"], dv_var)])
            V(lambda v: v.reciprocal(scr1, scr0), waits=[(sems["AC"], ac_std)])
            dv_rstd = V(lambda v: v.tensor_copy(rstd_v, scr1))
            pe_bc = 0
            for half in range(2):
                T(lambda t, _h=half: t.matmul(
                    meanB[:, _h * 512:(_h + 1) * 512], onesr[0:1, :],
                    mean_v[:, _h * 512:(_h + 1) * 512], start=True, stop=True),
                  waits=[(sems["DV"], dv_rstd)] if half == 0 else ())
                pe_bc = T(lambda t, _h=half: t.matmul(
                    rstdB[:, _h * 512:(_h + 1) * 512], onesr[0:1, :],
                    rstd_v[:, _h * 512:(_h + 1) * 512], start=True, stop=True),
                    inc=(half == 1))
            mb_sb = dstg[:, 11:13, :, :].rearrange("p b h j -> p (b h j)")
            rs_sb = dstg[:, 13:15, :, :].rearrange("p b h j -> p (b h j)")
            V(lambda v: v.tensor_copy(mb_sb, meanB), waits=[(sems["PE"], pe_bc)])
            V(lambda v: v.tensor_copy(rs_sb, rstdB))
            dvn = 0
            nhc = []
            for hc in range(HT):
                V(lambda v, _hc=hc: v.tensor_tensor(
                    tmp_n[:], xT[:, _hc, :], mb_sb, op=ALU.subtract))
                dvn = V(lambda v, _hc=hc: v.tensor_tensor(
                    lnb[:, _hc, :], tmp_n[:], rs_sb, op=ALU.mult))
                nhc.append(dvn)
            norm_done[ln_idx] = dvn
            norm_hc[ln_idx] = nhc
            tile_fin.clear()

        def emit_ffn(pass_idx, ln_idx, final=False):
            dseq = 0
            dbuf = {}
            dfin = {}
            pTT_by_ktl = {}
            pA_guard = [None, None]
            pB_guard = [None, None]
            guard_of = {}
            slot_seen = set()
            rot = [0]
            out_rel = {}
            pTT_last = 0
            next_ip = 0
            for gi, grp in enumerate(GRPS):
                Gn = len(grp)

                def emit_down_fills(_gi, _grp, _ck):
                    nonlocal dseq
                    for _ktl, _it in enumerate(_grp):
                        b = dseq % DPOOL
                        dbuf[(_gi, _ck, _ktl)] = b
                        w = [drel.get(b)]
                        if dseq < DPOOL and b >= 11:
                            w.append((sems["DV"], norm_done[ln_idx]))
                        dfin[(_gi, _ck, _ktl)] = DMA(
                            "g", dstg[:, b], dw[_it, :, 4 * _ck:4 * (_ck + 1), :],
                            f"D{b}", waits=w)
                        dseq += 1
                if gi > 0:
                    next_ip = 1

                def gu_epilogue(it, ip, pe_g, pe_u):
                    nonlocal pTT_last, guard_of
                    ac_g = A(lambda s, _it=it, _ip=ip: s.activation(
                        g_sb[:], psg[_ip], AF.Copy, scale=scmb[:, _it:_it + 1]),
                        waits=[(sems["PE"], pe_g), (sems["DV"], dv_consts),
                               gsb_rel[0]])
                    ktl = grp.index(it)
                    dv_p = V(lambda v, _ip=ip, _k=ktl: v.tensor_tensor(
                        pT[:, _k, :], g_sb[:], psu[_ip], op=ALU.mult),
                        waits=[(sems["AC"], ac_g), (sems["PE"], pe_u)])
                    pTT_last = dv_p
                    pTT_by_ktl[ktl] = dv_p
                    if ip == 0:
                        pA_guard[0] = (sems["AC"], ac_g)
                        pA_guard[1] = (sems["DV"], dv_p)
                    else:
                        pB_guard[0] = (sems["AC"], ac_g)
                        pB_guard[1] = (sems["DV"], dv_p)
                    guard_of = {0: pA_guard[0], 1: pA_guard[0],
                                2: pA_guard[1], 3: pA_guard[1],
                                4: pB_guard[0]}
                    gsb_rel[0] = (sems["DV"], dv_p)
                    slot_rel[ip * 2] = (sems["AC"], ac_g)
                    slot_rel[ip * 2 + 1] = (sems["DV"], dv_p)
                    if ktl < 3:
                        emit_down_fills(gi, grp, ktl)

                if gi == 0:
                    it0, it1 = grp[0], grp[1]
                    ip0, ip1 = next_ip, 1 - next_ip
                    pe_gs = {}
                    pe_us = {}
                    pend = {}
                    for h in range(4):
                        fw = {}
                        for tid, it in ((0, it0), (1, it1)):
                            b = (it * 4 + h) % 3
                            fw[tid] = (b, DMA(
                                "g", gustg[:, b],
                                guw[it, :, 8 * h:8 * (h + 1), :, :],
                                f"G{b}", waits=[gu_rel["G"].get(b)]))
                        for hcl in range(8):
                            hc = h * 8 + hcl
                            for tid, it, ipx in ((0, it0, ip0), (1, it1, ip1)):
                                b, wf = fw[tid]
                                for wname, wi, ps in (("G", 0, psg[ipx]),
                                                      ("U", 1, psu[ipx])):
                                    waits = []
                                    if hcl == 0 and wname == "G":
                                        waits.append(wf)
                                        if h == 0:
                                            waits.append(slot_rel[ipx * 2])
                                            waits.append(slot_rel[ipx * 2 + 1])
                                            waits.append(
                                                (sems["AC"], mst["last_sign_ac"]))
                                            waits.append(
                                                (sems["DV"], norm_hc[ln_idx][0]))
                                    if wname == "G" and tid == 0:
                                        waits.append(
                                            (sems["DV"], norm_hc[ln_idx][hc]))
                                    for s2 in range(2):
                                        pe = T(lambda t, _b=b, _hcl=hcl, _wi=wi,
                                               _hc=hc, _s2=s2, _ps=ps: t.matmul(
                                                   _ps[:, _s2 * 512:(_s2 + 1) * 512],
                                                   gustg[:, _b, _hcl, _wi, :],
                                                   lnb[:, _hc, _s2 * 512:(_s2 + 1) * 512],
                                                   start=(_hc == 0),
                                                   stop=(_hc == HT - 1)),
                                               waits=waits,
                                               inc=(hcl == 7 and s2 == 1))
                                        waits = []
                                    if hcl == 7:
                                        if wname == "G":
                                            pe_gs[tid] = pe
                                        else:
                                            pe_us[tid] = pe
                                            gu_rel["G"][b] = (sems["PE"], pe)
                    gu_epilogue(it0, ip0, pe_gs[0], pe_us[0])
                    gu_epilogue(it1, ip1, pe_gs[1], pe_us[1])
                    tail = grp[2:]
                else:
                    tail = grp
                for it in tail:
                    ip = next_ip
                    next_ip = 1 - next_ip
                    pe_g = pe_u = 0
                    for h in range(4):
                        b = (it * 4 + h) % 3
                        w_fill = DMA(
                            "g", gustg[:, b],
                            guw[it, :, 8 * h:8 * (h + 1), :, :],
                            f"G{b}", waits=[gu_rel["G"].get(b)])
                        waits = [w_fill]
                        if h == 0:
                            waits.append(slot_rel[ip * 2])
                            waits.append(slot_rel[ip * 2 + 1])
                            waits.append((sems["DV"], norm_done[ln_idx]))
                            waits.append((sems["AC"], mst["last_sign_ac"]))
                        for wname, wi, ps in (("G", 0, psg[ip]),
                                              ("U", 1, psu[ip])):
                            for hcl in range(8):
                                hc = h * 8 + hcl
                                for s2 in range(2):
                                    pe = T(lambda t, _b=b, _hcl=hcl, _wi=wi,
                                           _hc=hc, _s2=s2, _ps=ps: t.matmul(
                                               _ps[:, _s2 * 512:(_s2 + 1) * 512],
                                               gustg[:, _b, _hcl, _wi, :],
                                               lnb[:, _hc, _s2 * 512:(_s2 + 1) * 512],
                                               start=(_hc == 0),
                                               stop=(_hc == HT - 1)),
                                           waits=waits,
                                           inc=(hcl == 7 and s2 == 1))
                                    waits = []
                            if wname == "G":
                                pe_g = pe
                            else:
                                pe_u = pe
                                gu_rel["G"][b] = (sems["PE"], pe)
                    gu_epilogue(it, ip, pe_g, pe_u)
                mst["ac_floor"] = cnt["AC"]
                slot_seen = set()
                for qtr in range(8):
                    for hml in range(4):
                        hm = qtr * 4 + hml
                        is_fg = final and gi == len(GRPS) - 1
                        nslot = 5 if gi < len(GRPS) - 1 else 4
                        slot_of = {0: rot[0] % nslot, 1: (rot[0] + 1) % nslot}
                        rot[0] += 2
                        pe_s2 = [0, 0]
                        first = True
                        for ktl in range(Gn):
                            bix = dbuf[(gi, qtr, ktl)]
                            for s2 in range(2):
                                slot = slot_of[s2]
                                waits = []
                                if ktl == 0:
                                    waits.append(psd4_free[slot])
                                    if slot not in slot_seen:
                                        slot_seen.add(slot)
                                        waits.append(guard_of.get(slot))
                                    if first and hml == 0:
                                        for k2 in range(Gn):
                                            waits.append(dfin[(gi, qtr, k2)])
                                if qtr == 0 and hml == 0 and s2 == 0:
                                    waits.append((sems["DV"], pTT_by_ktl[ktl]))

                                first = False
                                pe = T(lambda t, _k=ktl, _s2=s2, _slot=slot,
                                       _hml=hml, _b=bix, _G=Gn: t.matmul(
                                           psd4[_slot],
                                           dstg[:, _b, _hml, :],
                                           pT[:, _k, _s2 * 512:(_s2 + 1) * 512],
                                           start=(_k == 0), stop=(_k == _G - 1)),
                                       waits=waits, inc=(ktl == Gn - 1))
                                if ktl == Gn - 1:
                                    pe_s2[s2] = pe
                        if hml == 3:
                            for ktl in range(Gn):
                                drel[dbuf[(gi, qtr, ktl)]] = (sems["PE"], pe_s2[1])
                        for s2 in range(2):
                            slot = slot_of[s2]
                            sl = slice(s2 * 512, (s2 + 1) * 512)
                            if is_fg:
                                dv_e = V(lambda v, _hm=hm, _slot=slot, _sl=sl:
                                         v.scalar_tensor_tensor(
                                             out_sb[:, _hm % 2, _sl], psd4[_slot],
                                             hds[:, _hm:_hm + 1], xT[:, _hm, _sl],
                                             op0=ALU.mult, op1=ALU.add),
                                         waits=[(sems["PE"], pe_s2[s2]),
                                                out_rel.get(hm % 2) if s2 == 0 else None])
                                psd4_free[slot] = (sems["DV"], dv_e)
                                if s2 == 1:
                                    out_rel[hm % 2] = DMA(
                                        "y", yt[hm], out_sb[:, hm % 2],
                                        f"O{hm % 2}", waits=[(sems["DV"], dv_e)])
                                    tile_fin[hm] = (sems["DV"], dv_e)
                            else:
                                dv_a = V(lambda v, _hm=hm, _slot=slot, _sl=sl:
                                         v.scalar_tensor_tensor(
                                             xT[:, _hm, _sl], psd4[_slot],
                                             hds[:, _hm:_hm + 1],
                                             xT[:, _hm, _sl],
                                             op0=ALU.mult, op1=ALU.add),
                                         waits=[(sems["PE"], pe_s2[s2])])
                                psd4_free[slot] = (sems["DV"], dv_a)
                                if s2 == 1:
                                    tile_fin[hm] = (sems["DV"], dv_a)
                        if pass_idx == 0 and (hm % 4) == 3:
                            if mst["li"] < 2:
                                emit_mask_load()
                                emit_mask_load()
                            else:
                                emit_mask_comp()
                                emit_mask_load()
                    if qtr < 5:
                        emit_down_fills(gi, grp, qtr + 3)
                    del qtr
                def _mx(a, b):
                    if a is None:
                        return b
                    if b is None:
                        return a
                    return a if a[1] >= b[1] else b
                slot_rel[0] = _mx(psd4_free[0], psd4_free[1])
                slot_rel[1] = _mx(psd4_free[2], psd4_free[3])
                slot_rel[2] = _mx(slot_rel[2], psd4_free[4])
                psd_free[0] = slot_rel[0]
                psd_free[1] = slot_rel[1]

        def emit_attn(ln_idx):
            w_all_wm0 = (sems["WMW0"], 16 * fills["WMW0"])
            w_all_wm1 = (sems["WMW1"], 16 * fills["WMW1"])
            ws_rel = {}
            slab = shp[:, 0:4096].rearrange("p (b kt j) -> p b kt j", b=2, j=128)
            for nt in range(HT):
                par = nt % 2
                src_nt = wmscr.rearrange("kt p (nt j) -> nt p kt j", j=128)[nt]
                w_s0 = DMA("y", slab[:, 0], src_nt[:, 0:16],
                           f"WS0",
                           waits=[w_all_wm0 if nt == 0 else None,
                                  w_all_wm1 if nt == 0 else None,
                                  ws_rel.get(0)])
                w_s1 = DMA("y", slab[:, 1], src_nt[:, 16:32], f"WS1",
                           waits=[ws_rel.get(1)])
                waits = [w_s0, psd_free[par]]
                if nt != 0:
                    waits.append((sems["DV"], norm_done[ln_idx]))
                pe_a = 0
                for kt in range(HT):
                    if kt == 16:
                        waits.append(w_s1)
                    if nt == 0:
                        waits.append((sems["DV"], norm_hc[ln_idx][kt]))
                    for s2 in range(2):
                        pe_a = T(lambda t, _kt=kt, _s2=s2, _par=par:
                                 t.matmul(
                                     psd[_par][:, _s2 * 512:(_s2 + 1) * 512],
                                     slab[:, _kt // 16, _kt % 16, :],
                                     lnb[:, _kt, _s2 * 512:(_s2 + 1) * 512],
                                     start=(_kt == 0), stop=(_kt == HT - 1)),
                                 waits=waits,
                                 inc=((kt == 15 or kt == HT - 1) and s2 == 1))
                        waits = []
                        if kt == 15 and s2 == 1:
                            ws_rel[0] = (sems["PE"], cnt["PE"])
                ws_rel[1] = (sems["PE"], pe_a)
                dv_e = V(lambda v, _nt=nt, _par=par: v.scalar_tensor_tensor(
                    xT[:, _nt, :], psd[_par], bi_s[:, _nt:_nt + 1],
                    xT[:, _nt, :], op0=ALU.add, op1=ALU.add),
                    waits=[(sems["PE"], pe_a)])
                tile_fin[nt] = (sems["DV"], dv_e)
                psd_free[par] = (sems["DV"], dv_e)
            slot_rel[0] = psd_free[0]
            slot_rel[1] = psd_free[1]
            psd4_free[0] = psd4_free[1] = psd_free[0]
            psd4_free[2] = psd4_free[3] = psd_free[1]

        emit_ln(0, first=True)
        emit_ffn(0, 0)
        while mst["ci"] < len(mask_units):
            emit_mask_comp()
            emit_mask_load()
        emit_ln(1)
        emit_attn(1)
        emit_ln(2)
        emit_ffn(1, 2, final=True)
        YW([(sems["O0"], 16 * fills["O0"]), (sems["O1"], 16 * fills["O1"])])

        @block.tensor
        def _(t):
            for f in prog["t"]:
                f(t)

        @block.vector
        def _(v):
            for f in prog["v"]:
                f(v)

        @block.scalar
        def _(s):
            for f in prog["s"]:
                f(s)

        @block.gpsimd
        def _(g):
            for f in prog["g"]:
                f(g)

        @block.sync
        def _(y):
            for f in prog["y"]:
                f(y)

    return nc


_NC_CACHE = None


def _get_nc():
    global _NC_CACHE
    if _NC_CACHE is None:
        _NC_CACHE = build_nc()
    return _NC_CACHE


def kernel(L0_attn_data, L0_attn_mask, L0_attn_weight, L0_attn_bias,
           L0_ffn_gate_wq, L0_ffn_gate_scale, L0_ffn_up_wq, L0_ffn_up_scale,
           L0_ffn_down_wq, L0_ffn_down_scale):
    x = np.ascontiguousarray(np.asarray(L0_attn_data, dtype=np.float32)
                             ).reshape(8192, 4096)
    mask = np.ascontiguousarray(np.asarray(L0_attn_mask, dtype=np.int32))
    weight = np.ascontiguousarray(np.asarray(L0_attn_weight, dtype=np.float32))
    bias = np.asarray(L0_attn_bias, dtype=np.float32)
    gwq = np.asarray(L0_ffn_gate_wq, dtype=np.int32)
    uwq = np.asarray(L0_ffn_up_wq, dtype=np.int32)
    dwq = np.asarray(L0_ffn_down_wq, dtype=np.int32)
    gsc = np.asarray(L0_ffn_gate_scale, dtype=np.float32)
    usc = np.asarray(L0_ffn_up_scale, dtype=np.float32)
    dsc = np.asarray(L0_ffn_down_scale, dtype=np.float32)

    gw = gwq.reshape(IT, 128, HT, 128).transpose(0, 3, 2, 1)
    uw = uwq.reshape(IT, 128, HT, 128).transpose(0, 3, 2, 1)
    guw = np.ascontiguousarray(np.stack((gw, uw), axis=3).astype(np.int8))
    dw = np.ascontiguousarray(
        dwq.reshape(HT, 128, IT, 128).transpose(2, 3, 0, 1).astype(np.int8))
    aww = np.ascontiguousarray(weight.reshape(HT, 128, HT, 128).transpose(2, 3, 0, 1))
    mkk = np.ascontiguousarray(mask.T.astype(np.int8)).reshape(HT, 128, 4096)
    gs2 = np.ascontiguousarray(gsc.reshape(IT, 128).T)
    us2 = np.ascontiguousarray(usc.reshape(IT, 128).T)
    ds2 = np.ascontiguousarray(dsc.reshape(HT, 128).T)
    bi2 = np.ascontiguousarray(bias.reshape(HT, 128).T)
    onesb = np.ones((128, 1), ml_dtypes.bfloat16)
    onesr = np.ones((1, 128), ml_dtypes.bfloat16)
    a128 = (np.arange(128)[:, None] // 32 == np.arange(128)[None, :] // 32
            ).astype(np.float32)

    nc = _get_nc()
    in_maps = []
    for c in range(NCORE):
        xs = np.ascontiguousarray(x[c * S:(c + 1) * S, :].T).reshape(HT, 128, S)
        in_maps.append({"xin": xs, "guw": guw, "dw": dw, "aw": aww,
                        "mk": mkk, "gs": gs2, "us": us2, "ds": ds2, "bi": bi2,
                        "onesb": onesb, "onesr": onesr, "a128": a128})
    res = run_bass_kernel_spmd(nc, in_maps, core_ids=list(range(NCORE)))
    out = np.empty((8192, 4096), np.float32)
    for c in range(NCORE):
        ytc = np.asarray(res.results[c]["yt"])
        out[c * S:(c + 1) * S, :] = ytc.reshape(4096, S).T
    return out.reshape(1, 8192, 4096)
